# version 01
# baseline (speedup 1.0000x reference)
import numpy as np
import ml_dtypes
import concourse.bass as bass
import concourse.mybir as mybir
from concourse.bass_utils import run_bass_kernel_spmd

F32 = mybir.dt.float32
F32R = mybir.dt.float32r
BF16 = mybir.dt.bfloat16
F8 = mybir.dt.float8e4
AF = mybir.ActivationFunctionType
ALU = mybir.AluOpType

W = 1024
H = 16
A = 64
NC = 8
NCHUNK = 8
T = 128


class _St:
    __slots__ = ("name", "eng", "fn", "deps", "sem", "amt", "val", "is_dma")


def build_program(n_seq=2, n_tiles=16, debug=False):
    n_tok = n_tiles * T
    nc = bass.Bass("TRN2", target_bir_lowering=False, detect_race_conditions=False)

    dX = nc.dram_tensor("x", [n_seq, n_tok, W], F32, kind="ExternalInput").ap()
    dOut = nc.dram_tensor("out", [n_seq, n_tok, W], F32, kind="ExternalOutput").ap()
    dVw = nc.dram_tensor("vw", [W, 1024], F32, kind="ExternalInput").ap()
    dWo = nc.dram_tensor("wo", [1024, W], F32, kind="ExternalInput").ap()
    dKc = nc.dram_tensor("kc", [W, 48], F32, kind="ExternalInput").ap()
    dFk = nc.dram_tensor("fk", [W, 1024], BF16, kind="ExternalInput").ap()
    dBias = nc.dram_tensor("bias_rep", [128, 1024], BF16, kind="ExternalInput").ap()
    dTri = nc.dram_tensor("tri", [128, 128], F32R, kind="ExternalInput").ap()
    dIden = nc.dram_tensor("iden", [128, 128], F32, kind="ExternalInput").ap()
    dOnes1 = nc.dram_tensor("ones1", [1, 128], F32, kind="ExternalInput").ap()
    dOnes2 = nc.dram_tensor("ones2", [2, 128], F32R, kind="ExternalInput").ap()
    dNrep = nc.dram_tensor("n_rep", [128, 2048], F32, kind="ExternalInput").ap()
    dA1 = nc.dram_tensor("a1c", [128, 8], F32, kind="ExternalInput").ap()
    dB1 = nc.dram_tensor("b1c", [128, 8], F32, kind="ExternalInput").ap()
    dA2 = nc.dram_tensor("a2c", [128, 8], F32, kind="ExternalInput").ap()
    dB2 = nc.dram_tensor("b2c", [128, 8], F32, kind="ExternalInput").ap()
    dCc = nc.dram_tensor("cc", [128, 8], F32, kind="ExternalInput").ap()
    dInd = nc.dram_tensor("ind", [128, 8, 16], F32, kind="ExternalInput").ap()
    dSin = nc.dram_tensor("s_in", [1, 1], F32, kind="ExternalInput").ap()
    dSk = nc.dram_tensor("s_k", [1, 1], F32, kind="ExternalInput").ap()

    dbg = {}
    if debug:
        for nm, shp, dt in [("d_xn", [128, 1024], F32), ("d_gates", [128, 48], F32),
                            ("d_S", [128, 1040], F32), ("d_Q", [128, 1024], F32),
                            ("d_V", [128, 1024], F32), ("d_p1", [128, 768], F32),
                            ("d_p2e", [128, 256], F32), ("d_ffn", [128, 1024], F32),
                            ("d_den", [128, 64], F32)]:
            dbg[nm] = nc.dram_tensor(nm, shp, dt, kind="ExternalOutput").ap()

    sb = nc.alloc_sbuf_tensor
    vw_hi = sb("vw_hi", [128, 8, 1024], F32R).ap()
    vw_lo = sb("vw_lo", [128, 8, 1024], BF16).ap()
    wo_hi = sb("wo_hi", [128, 8, 1024], F32R).ap()
    wo_lo = sb("wo_lo", [128, 8, 1024], BF16).ap()
    kq = sb("kq", [128, 8, 1024], F8).ap()
    kc_hi = sb("kc_hi", [128, 8, 48], F32R).ap()
    kc_lo = sb("kc_lo", [128, 8, 48], BF16).ap()
    kc_f32 = sb("kc_f32", [128, 8, 48], F32).ap()
    wst = sb("wst", [128, 2, 1024], F32).ap()
    fst = sb("fst", [128, 2, 1024], BF16).ap()

    xb = [sb(f"xb{i}", [128, 1024], F32).ap() for i in range(2)]
    ob = [sb(f"ob{i}", [128, 1024], F32).ap() for i in range(2)]
    xc_t = sb("xc_t", [128, 1024], F32).ap()
    xn_tm = sb("xn_tm", [128, 1024], F32).ap()
    sums = sb("sums", [128, 1], F32).ap()
    negmean = sb("negmean", [128, 1], F32).ap()
    varsum = sb("varsum", [128, 1], F32).ap()
    veps = sb("veps", [128, 1], F32).ap()
    lnv = sb("lnv", [128, 1], F32).ap()
    rstd = sb("rstd", [128, 1], F32).ap()
    xnF_hi = sb("xnF_hi", [128, 8, 128], F32R).ap()
    xnF_lo = sb("xnF_lo", [128, 8, 128], F32R).ap()
    t_bf = sb("t_bf", [128, 8, 128], BF16).ap()
    xq = sb("xq", [128, 8, 128], F8).ap()
    gates = sb("gates", [128, 48], F32).ap()
    g1 = sb("g1", [128, 48], F32).ap()
    Ee = sb("Ee", [128, 16], F32).ap()
    E2 = sb("E2", [128, 16], F32).ap()
    D2 = sb("D2", [128, 16], F32).ap()
    den = sb("den", [128, 16], F32).ap()
    rden = sb("rden", [128, 16], F32).ap()
    G_f32 = sb("G_f32", [128, 1040], F32).ap()
    G_hi = sb("G_hi", [128, 1040], F32R).ap()
    G_lo = sb("G_lo", [128, 1040], F32R).ap()
    t2 = G_f32[:, 0:1024]
    t3 = xn_tm
    QF_hi = sb("QF_hi", [128, 8, 128], F32R).ap()
    QF_lo = sb("QF_lo", [128, 8, 128], F32R).ap()
    q_bf = t_bf
    yb = sb("yb", [128, 1024], BF16).ap()
    yb2 = sb("yb2", [128, 1024], BF16).ap()
    r1 = G_f32[:, 0:1024]
    carry2 = sb("carry2", [2, 1040], F32R).ap()
    p1pad = sb("p1pad", [128, 16 * 48], F32).ap()
    p2e_t = sb("p2e_t", [128, 16 * 16], F32).ap()
    bias_rep = sb("bias_rep_s", [128, 1024], BF16).ap()
    tri = sb("tri_s", [128, 128], F32R).ap()
    iden = sb("iden_s", [128, 128], F32).ap()
    ones1 = sb("ones1_s", [1, 128], F32).ap()
    ones2 = sb("ones2_s", [2, 128], F32R).ap()
    n_rep = sb("n_rep_s", [128, 2048], F32).ap()
    a1c = sb("a1c_s", [128, 8], F32).ap()
    b1c = sb("b1c_s", [128, 8], F32).ap()
    a2c = sb("a2c_s", [128, 8], F32).ap()
    b2c = sb("b2c_s", [128, 8], F32).ap()
    ccs = sb("cc_s", [128, 8], F32).ap()
    ind = sb("ind_s", [128, 8, 16], F32).ap()
    csel = sb("csel", [128, 8, 16], F32).ap()
    sinbuf = sb("sinbuf", [128, 1024], F32).ap()
    p_hm = sinbuf[0:16, :]
    wst_flat = wst.rearrange("p c m -> p (c m)")
    s_in_b = sb("s_in_b", [128, 1], F32).ap()
    s_k_b = sb("s_k_b", [128, 1], F32).ap()
    r_sin = sb("r_sin", [128, 1], F32).ap()
    r_sk = sb("r_sk", [128, 1], F32).ap()
    sisk = sb("sisk", [128, 1], F32).ap()
    s_st = sb("s_st", [1, 2], F32).ap()

    pAB = nc.alloc_psum_tensor("pAB", [128, 1024], F32).ap()
    pCD = nc.alloc_psum_tensor("pCD", [128, 1024], F32).ap()
    pE = nc.alloc_psum_tensor("pE", [128, 512], F32).ap()
    pFG = nc.alloc_psum_tensor("pFG", [128, 1024], F32).ap()
    pH = nc.alloc_psum_tensor("pH", [128, 512], F32).ap()

    semT = nc.alloc_semaphore("semT")
    semV = nc.alloc_semaphore("semV")
    semS = nc.alloc_semaphore("semS")
    semG = nc.alloc_semaphore("semG")
    semX = [nc.alloc_semaphore(f"semX{i}") for i in range(2)]
    semWc = nc.alloc_semaphore("semWc")
    semWw = [nc.alloc_semaphore(f"semWw{i}") for i in range(2)]
    semWf = [nc.alloc_semaphore(f"semWf{i}") for i in range(2)]
    semO = [nc.alloc_semaphore(f"semO{i}") for i in range(2)]
    semC = nc.alloc_semaphore("semC")
    semD = nc.alloc_semaphore("semD")
    semA = nc.alloc_semaphore("semA")
    _aux = [0]

    def _guard(eng, producer_inst):
        _aux[0] += 1
        producer_inst.then_inc(semA, 1)
        eng.wait_ge(semA, _aux[0])
    sems = {"tensor": semT, "vector": semV, "scalar": semS, "gpsimd": semG}

    stages = []
    from collections import defaultdict
    ctr = defaultdict(int)

    def stage(name, eng, fn, deps=(), sem=None, amt=1, dma=False):
        st = _St()
        st.name, st.eng, st.fn = name, eng, fn
        st.deps = [d for d in deps if d is not None]
        st.sem = sem if sem is not None else sems[eng]
        st.amt, st.is_dma = amt, dma
        ctr[id(st.sem)] += amt
        st.val = ctr[id(st.sem)]
        stages.append(st)
        DBG_TRACE.append((name, eng))
        return st

    st_const = st_bcast = st_scalprep = st_kq = st_kc_hi = st_kc_lo = None
    st_wsplit_done = st_csel = st_p1ev = st_ptev = None
    if not DBG_STARTUP[0]:
        stages_startup_skipped = True
    def f_const_loads(eng, st):
        for dst_, src_ in [(tri, dTri), (iden, dIden), (ones1, dOnes1), (ones2, dOnes2),
                           (n_rep, dNrep), (a1c, dA1), (b1c, dB1), (a2c, dA2),
                           (b2c, dB2), (ccs, dCc), (ind, dInd),
                           (bias_rep, dBias),
                           (kc_f32, dKc.rearrange("(c p) k -> p c k", p=128))]:
            eng.dma_start(out=dst_, in_=src_).then_inc(st.sem, 16)
        eng.dma_start(out=s_st[:, 0:1], in_=dSin).then_inc(st.sem, 16)
        eng.dma_start(out=s_st[:, 1:2], in_=dSk).then_inc(st.sem, 16)
    if DBG_STARTUP[0]:
        st_const = stage("const_loads", "scalar", f_const_loads, sem=semWc, amt=16 * 15, dma=True)

    if DBG_STARTUP[0]:
        def f_bcast(eng, st):
            eng.matmul(pH[:, 0:1], ones1, s_st[:, 0:1], start=True, stop=True)
            return eng.matmul(pH[:, 1:2], ones1, s_st[:, 1:2], start=True, stop=True)
        st_bcast = stage("bcast", "tensor", f_bcast, deps=[st_const])

        def f_scalprep(eng, st):
            eng.tensor_copy(s_in_b, pH[:, 0:1])
            i_c = eng.tensor_copy(s_k_b, pH[:, 1:2])
            _guard(eng, i_c)
            i_r1 = eng.reciprocal(r_sin, s_in_b)
            _guard(eng, i_r1)
            i_r2 = eng.reciprocal(r_sk, s_k_b)
            _guard(eng, i_r2)
            eng.tensor_mul(sisk, s_in_b, s_k_b)
            eng.memset(p1pad, 0.0)
            return eng.memset(carry2.bitcast(F32), 0.0)
        st_scalprep = stage("scalprep", "vector", f_scalprep, deps=[st_bcast])

        st_fdma, st_kq_c = {}, {}
        for c in range(8):
            def f_fdma(eng, st, c=c):
                eng.dma_start(out=fst[:, c % 2, :],
                              in_=dFk[c * 128:(c + 1) * 128, :]).then_inc(st.sem, 16)
            st_fdma[c] = stage(f"fdma{c}", "scalar", f_fdma,
                               deps=[st_kq_c.get(c - 2)], sem=semWf[c % 2], amt=16, dma=True)

            def f_kq(eng, st, c=c):
                return eng.activation(kq[:, c, :], fst[:, c % 2, :], AF.Copy, scale=r_sk)
            st_kq_c[c] = stage(f"kq{c}", "scalar", f_kq, deps=[st_fdma[c], st_scalprep])
        st_kq = st_kq_c[7]

        def f_kc_hi(eng, st):
            return eng.tensor_copy(kc_hi, kc_f32)
        st_kc_hi = stage("kc_hi", "gpsimd", f_kc_hi, deps=[st_const])

        def f_kc_lo(eng, st):
            return eng.tensor_sub(kc_lo, kc_f32, kc_hi)
        st_kc_lo = stage("kc_lo", "vector", f_kc_lo, deps=[st_kc_hi, st_scalprep])

        st_wdma, st_whi, st_wlo = {}, {}, {}
        for gi, (dsrc, whi, wlo) in enumerate([(dVw, vw_hi, vw_lo), (dWo, wo_hi, wo_lo)]):
            for c in range(8):
                i = gi * 8 + c
                par = i % 2
                prev_lo = st_wlo.get(i - 2)

                def f_wdma(eng, st, dsrc=dsrc, c=c, par=par):
                    eng.dma_start(out=wst[:, par, :],
                                  in_=dsrc[c * 128:(c + 1) * 128, :]).then_inc(st.sem, 16)
                st_wdma[i] = stage(f"wdma{i}", "scalar", f_wdma,
                                   deps=[prev_lo], sem=semWw[i % 2], amt=16, dma=True)

                def f_whi(eng, st, whi=whi, c=c, par=par):
                    return eng.activation(whi[:, c, :], wst[:, par, :], AF.Copy)
                st_whi[i] = stage(f"whi{i}", "scalar", f_whi, deps=[st_wdma[i]])

                def f_wlo(eng, st, whi=whi, wlo=wlo, c=c, par=par):
                    return eng.tensor_sub(wlo[:, c, :], wst[:, par, :], whi[:, c, :])
                st_wlo[i] = stage(f"wlo{i}", "vector", f_wlo, deps=[st_whi[i]])
        st_wsplit_done = st_wlo[15]

        def f_csel(eng, st):
            for c in range(8):
                last = eng.tensor_scalar(csel[:, c, :], ind[:, c, :], ccs[:, c:c + 1],
                                         None, ALU.mult)
            return last
        st_csel = stage("csel", "vector", f_csel, deps=[st_const])

        st_sin, st_pred = {}, {}
        st_p1ev = None
        st_ptr_prev = None
        p1_slices = [pAB[0:16, 0:512], pAB[0:16, 512:1024], pCD[0:16, 0:512], pCD[0:16, 512:1024]]
        st_ptev = None
        for pi, (ac, bc) in enumerate([(a1c, b1c), (a2c, b2c)]):
            for c8 in range(16):
                c, half = divmod(c8, 2)
                prev_pred = (st_pred.get((pi, c8 - 1))
                             or (st_pred.get((pi - 1, 15)) if pi else None))

                def f_sin(eng, st, ac=ac, bc=bc, c=c, half=half):
                    return eng.activation(sinbuf, n_rep[:, half * 1024:(half + 1) * 1024],
                                          AF.Sin, scale=ac[:, c:c + 1], bias=bc[:, c:c + 1])
                st_sin[(pi, c8)] = stage(f"sin{pi}_{c8}", "scalar", f_sin,
                                         deps=[st_const, prev_pred,
                                               st_p1ev if (pi == 1 and c8 == 0) else None])

                def f_pred(eng, st, pi=pi, c=c, half=half):
                    for q in range(2):
                        last = eng.matmul(p1_slices[half * 2 + q], csel[:, c, :],
                                          sinbuf[:, q * 512:(q + 1) * 512],
                                          start=(c == 0), stop=(c == 7))
                    return last
                st_pred[(pi, c8)] = stage(f"pred{pi}_{c8}", "tensor", f_pred,
                                          deps=[st_sin[(pi, c8)], st_csel,
                                                st_bcast if (pi == 0 and c8 == 0) else None,
                                                st_p1ev if (pi == 1 and c8 == 0) else None])
            for half in range(2):
                def f_pev(eng, st, half=half):
                    eng.tensor_copy(p_hm[:, 0:512], p1_slices[half * 2])
                    return eng.tensor_copy(p_hm[:, 512:1024], p1_slices[half * 2 + 1])
                st_pev = stage(f"pev{pi}_{half}", "vector", f_pev,
                               deps=[st_pred[(pi, 15)], st_ptr_prev])
                if pi == 1:
                    def f_p2clip(eng, st):
                        return eng.tensor_scalar(p_hm, p_hm, 20.0, -20.0, ALU.min, ALU.max)
                    st_p2clip = stage(f"p2clip{half}", "vector", f_p2clip, deps=[st_pev])

                    def f_p2exp(eng, st):
                        return eng.activation(p_hm, p_hm, AF.Exp)
                    st_ready = stage(f"p2exp{half}", "scalar", f_p2exp, deps=[st_p2clip])
                else:
                    st_ready = st_pev

                def f_ptr(eng, st):
                    for j in range(8):
                        last = eng.transpose(pH[:, j * 16:(j + 1) * 16],
                                             p_hm[:, j * 128:(j + 1) * 128],
                                             iden[0:16, 0:16])
                    return last
                st_ptr_prev = stage(f"ptr{pi}_{half}", "tensor", f_ptr,
                                    deps=[st_ready, st_ptev])

                def f_ptev(eng, st, pi=pi, half=half):
                    for j8 in range(8):
                        j = half * 8 + j8
                        if pi == 0:
                            last = eng.tensor_copy(p1pad[:, j * 48:j * 48 + 16],
                                                   pH[:, j8 * 16:(j8 + 1) * 16])
                        else:
                            last = eng.tensor_copy(p2e_t[:, j * 16:(j + 1) * 16],
                                                   pH[:, j8 * 16:(j8 + 1) * 16])
                    return last
                st_ptev = stage(f"ptev{pi}_{half}", "vector", f_ptev,
                                deps=[st_ptr_prev, st_scalprep])
            if pi == 0:
                st_p1ev = st_ptev

        if debug:
            def f_dbg_p(eng, st):
                eng.dma_start(out=dbg["d_p1"], in_=p1pad).then_inc(st.sem, 16)
                eng.dma_start(out=dbg["d_p2e"], in_=p2e_t).then_inc(st.sem, 16)
            stage("dbg_p", "gpsimd", f_dbg_p, deps=[st_ptev], sem=semD, amt=32, dma=True)

    n_tiles_total = n_seq * n_tiles
    prev = {}

    for u in range(n_tiles_total):
        s, j = divmod(u, n_tiles)
        xbuf, obuf = xb[u % 2], ob[u % 2]
        first_of_seq = (j == 0)

        def f_ldx(eng, st, s=s, j=j, xbuf=xbuf):
            eng.dma_start(out=xbuf, in_=dX[s, j * T:(j + 1) * T, :]).then_inc(st.sem, 16)
        st_ldx = stage(f"ldx{u}", "sync", f_ldx,
                       deps=[prev.get(("res", u - 2))], sem=semX[u % 2], amt=16, dma=True)

        def f_ln1(eng, st, xbuf=xbuf):
            i_sum = eng.reduce_sum(sums, xbuf, mybir.AxisListType.X)
            _guard(eng, i_sum)
            i_nm = eng.tensor_scalar(negmean, sums, -1.0 / 1024.0, None, ALU.mult)
            _guard(eng, i_nm)
            eng.tensor_scalar(xc_t, xbuf, negmean, None, ALU.add)
            i_sq = eng.scalar_tensor_tensor(xn_tm, xc_t, 1.0, xc_t, ALU.mult, ALU.mult,
                                            accum_out=varsum)
            _guard(eng, i_sq)
            return eng.tensor_scalar(veps, varsum, 1.0 / 1024.0, 1e-5, ALU.mult, ALU.add)
        st_ln1 = stage(f"ln1{u}", "vector", f_ln1,
                       deps=[st_ldx, st_ptev if u == 0 else None,
                             prev.get(("qtr", u - 1))])

        def f_ln2(eng, st):
            eng.activation(lnv, veps, AF.Ln)
            return eng.activation(rstd, lnv, AF.Exp, scale=-0.5)
        st_ln2 = stage(f"ln2{u}", "scalar", f_ln2, deps=[st_ln1])

        def f_ln3(eng, st):
            return eng.tensor_scalar(xn_tm, xc_t, rstd, None, ALU.mult)
        st_ln3 = stage(f"ln3{u}", "vector", f_ln3, deps=[st_ln2])

        st_dbgxn = None
        if debug and u == DEBUG_TILE[0]:
            def f_dbg_xn(eng, st):
                eng.dma_start(out=dbg["d_xn"], in_=xn_tm).then_inc(st.sem, 16)
            st_dbgxn = stage(f"dbgxn{u}", "gpsimd", f_dbg_xn, deps=[st_ln3],
                             sem=semD, amt=16, dma=True)

        if DBG_MODE[0] in ("ln", "copy", "ln1"):
            def f_res_ln(eng, st, obuf=obuf, xbuf=xbuf):
                if DBG_MODE[0] == "copy":
                    return eng.tensor_copy(obuf, xbuf)
                if DBG_MODE[0] == "ln1":
                    eng.tensor_copy(obuf, xc_t)
                    eng.tensor_copy(obuf[:, 0:1], sums)
                    eng.tensor_copy(obuf[:, 1:2], varsum)
                    eng.tensor_copy(obuf[:, 2:3], veps)
                    return eng.tensor_copy(obuf[:, 3:4], rstd)
                return eng.tensor_copy(obuf, xn_tm)
            st_res = stage(f"res{u}", "vector", f_res_ln,
                           deps=[st_ln3, prev.get(("st", u - 2)), st_dbgxn])
            prev[("res", u)] = st_res

            def f_st_ln(eng, st, s=s, j=j, obuf=obuf):
                eng.dma_start(out=dOut[s, j * T:(j + 1) * T, :], in_=obuf).then_inc(st.sem, 16)
            st_st = stage(f"st{u}", "sync", f_st_ln, deps=[st_res],
                          sem=semO[u % 2], amt=16, dma=True)
            prev[("st", u)] = st_st
            continue

        def f_tr(eng, st):
            for k in range(8):
                last = eng.transpose(pAB[:, k * 128:(k + 1) * 128],
                                     xn_tm[:, k * 128:(k + 1) * 128], iden)
            return last
        st_tr = stage(f"tr{u}", "tensor", f_tr,
                      deps=[st_ln3, prev.get(("ffne", u - 1)),
                            st_ptev if u == 0 else None])

        def f_hi(eng, st):
            eng.activation(xnF_hi.rearrange("p c t -> p (c t)"), pAB[:], AF.Copy)
            eng.activation(t_bf.rearrange("p c t -> p (c t)"), pAB[:], AF.Copy)
            return eng.activation(xq.rearrange("p c t -> p (c t)"),
                                  t_bf.rearrange("p c t -> p (c t)"), AF.Copy,
                                  scale=r_sin)
        st_hi = stage(f"hi{u}", "scalar", f_hi,
                      deps=[st_tr, prev.get(("vz", u - 1))])

        def f_lo(eng, st):
            return eng.tensor_sub(xnF_lo.rearrange("p c t -> p (c t)"), pAB[:],
                                  xnF_hi.rearrange("p c t -> p (c t)"))
        st_lo = stage(f"lo{u}", "vector", f_lo,
                      deps=[st_tr, st_hi, prev.get(("vz", u - 1))])

        def f_ffn(eng, st):
            for k in range(8):
                for b in range(2):
                    last = eng.matmul(pAB[:, b * 512:(b + 1) * 512], xq[:, k, :],
                                      kq[:, k, b * 512:(b + 1) * 512],
                                      start=(k == 0), stop=(k == 7))
            return last
        st_ffn = stage(f"ffn{u}", "tensor", f_ffn, deps=[st_hi, st_lo, st_kq])

        def f_vz(eng, st):
            last = None
            for p in range(3):
                xa = (xnF_hi, xnF_lo, t_bf)[p]
                wv = (vw_hi, vw_hi, vw_lo)[p]
                wk = (kc_hi, kc_hi, kc_lo)[p]
                for k in range(8):
                    first = (p == 0 and k == 0)
                    lastf = (p == 2 and k == 7)
                    for b in range(2):
                        last = eng.matmul(pCD[:, b * 512:(b + 1) * 512], xa[:, k, :],
                                          wv[:, k, b * 512:(b + 1) * 512],
                                          start=first, stop=lastf)
                    last = eng.matmul(pE[:, 0:48], xa[:, k, :], wk[:, k, :],
                                      start=first, stop=lastf)
            return last
        st_vz = stage(f"vz{u}", "tensor", f_vz,
                      deps=[st_hi, st_lo, st_wsplit_done, st_kc_lo,
                            prev.get(("res", u - 1)), prev.get(("g", u - 1)),
                            prev.get(("den", u - 1)), prev.get(("carry", u - 1)),
                            st_ptev if u == 0 else None])
        prev[("vz", u)] = st_vz

        def f_g1(eng, st, j=j):
            i_a = eng.tensor_add(g1, pE[:, 0:48], p1pad[:, j * 48:(j + 1) * 48])
            _guard(eng, i_a)
            return eng.tensor_scalar(g1, g1, 20.0, -20.0, ALU.min, ALU.max)
        st_g1 = stage(f"g1{u}", "vector", f_g1, deps=[st_vz])

        def f_g2(eng, st):
            return eng.activation(gates, g1, AF.Exp)
        st_g2 = stage(f"g2{u}", "scalar", f_g2, deps=[st_g1])

        if debug and u == DEBUG_TILE[0]:
            def f_dbg_g(eng, st):
                eng.dma_start(out=dbg["d_gates"], in_=gates).then_inc(st.sem, 16)
            stage(f"dbgg{u}", "gpsimd", f_dbg_g, deps=[st_g2], sem=semD, amt=16, dma=True)

        def f_G(eng, st, j=j):
            i_ee = eng.tensor_mul(Ee, gates[:, 32:48], p2e_t[:, j * 16:(j + 1) * 16])
            _guard(eng, i_ee)
            eng.tensor_mul(G_f32[:, 0:1024].rearrange("p (h a) -> p h a", h=16),
                           pCD[:].rearrange("p (h a) -> p h a", h=16),
                           gates[:, 0:16].unsqueeze(2).broadcast_to([128, 16, 64]))
            return eng.tensor_copy(G_f32[:, 1024:1040], gates[:, 0:16])
        st_G = stage(f"G{u}", "vector", f_G,
                     deps=[st_g2, prev.get(("ghl", u - 1))])

        st_dbgV_dma = None
        if debug and u == DEBUG_TILE[0]:
            def f_dbg_V(eng, st):
                return eng.tensor_copy(sinbuf, pCD[:])
            st_dbgV = stage(f"dbgVc{u}", "vector", f_dbg_V, deps=[st_G])
            def f_dbg_V2(eng, st):
                eng.dma_start(out=dbg["d_V"], in_=sinbuf).then_inc(st.sem, 16)
            st_dbgV_dma = stage(f"dbgV{u}", "gpsimd", f_dbg_V2, deps=[st_dbgV],
                                sem=semD, amt=16, dma=True)

        def f_Ghl(eng, st):
            eng.tensor_copy(G_hi, G_f32)
            return eng.tensor_sub(G_lo, G_f32, G_hi)
        st_Ghl = stage(f"Ghl{u}", "gpsimd", f_Ghl,
                       deps=[st_G, prev.get(("scan", u - 1)),
                             prev.get(("carry", u - 1))])
        prev[("ghl", u)] = st_Ghl

        def f_scan(eng, st, first_of_seq=first_of_seq):
            cuts = [(0, 512, pFG[:, 0:512]), (512, 1024, pFG[:, 512:1024]),
                    (1024, 1040, pH[:, 0:16])]
            npass = 2 if (first_of_seq or DBG_NO_CARRYMM[0]) else 3
            last = None
            for p in range(npass):
                for (c0, c1, dst) in cuts:
                    if p < 2:
                        src = (G_hi, G_lo)[p]
                        last = eng.matmul(dst, tri, src[:, c0:c1],
                                          start=(p == 0), stop=(p == npass - 1))
                    else:
                        last = eng.matmul(dst, ones2, carry2[:, c0:c1],
                                          start=False, stop=True)
            return eng.matmul(pE[0:1, 64:66], t_bf[:, 0, 0:1], t_bf[:, 0, 0:2],
                              start=True, stop=True)
        st_scan = stage(f"scan{u}", "tensor", f_scan,
                        deps=[st_Ghl, prev.get(("carry", u - 1)),
                              prev.get(("qa", u - 1)), prev.get(("den", u - 1)),
                              prev.get(("qhi", u - 1)), prev.get(("qlo", u - 1)),
                              st_ptev if u == 0 else None])
        prev[("scan", u)] = st_scan

        if j < n_tiles - 1:
            def f_carx(eng, st):
                eng.tensor_copy(n_rep[96:128, 0:1024], pFG[96:128, :])
                i_c = eng.tensor_copy(n_rep[96:128, 1024:1040], pH[96:128, 0:16])
                _guard(eng, i_c)
                i_h = eng.tensor_copy(G_hi[96:128, :], n_rep[96:128, 0:1040])
                _guard(eng, i_h)
                return eng.tensor_sub(G_lo[96:128, :], n_rep[96:128, 0:1040],
                                      G_hi[96:128, :])
            st_carx = stage(f"carx{u}", "vector", f_carx,
                            deps=[st_scan, st_ptev if u == 0 else None])

            def f_cardma(eng, st):
                eng.dma_start(out=carry2[0:1, :],
                              in_=G_hi[127:128, 0:1040]).then_inc(st.sem, 16)
                eng.dma_start(out=carry2[1:2, :],
                              in_=G_lo[127:128, 0:1040]).then_inc(st.sem, 16)
            st_carry = stage(f"cardma{u}", "gpsimd", f_cardma, deps=[st_carx],
                             sem=semC, amt=32, dma=True)
        else:
            def f_carx0(eng, st):
                return eng.memset(carry2.bitcast(F32), 0.0)
            st_carx = stage(f"carx{u}", "vector", f_carx0, deps=[st_scan])
            st_carry = st_carx
        prev[("carry", u)] = st_carry
        prev[("carx", u)] = st_carx

        st_dbgS_dma = None
        if debug and u == DEBUG_TILE[0]:
            def f_dbg_Sc(eng, st):
                eng.tensor_copy(wst_flat[:, 0:512], pFG[:, 0:512])
                eng.tensor_copy(wst_flat[:, 512:1024], pFG[:, 512:1024])
                return eng.tensor_copy(wst_flat[:, 1024:1040], pH[:, 0:16])
            st_dbgSc = stage(f"dbgSc{u}", "vector", f_dbg_Sc, deps=[st_scan])
            def f_dbg_S(eng, st):
                eng.dma_start(out=dbg["d_S"], in_=wst_flat[:, 0:1040]).then_inc(st.sem, 16)
            st_dbgS_dma = stage(f"dbgS{u}", "gpsimd", f_dbg_S, deps=[st_dbgSc],
                                sem=semD, amt=16, dma=True)

        def f_den(eng, st):
            i1 = eng.scalar_tensor_tensor(den, pH[:, 0:16], 1.0, Ee, ALU.mult, ALU.mult)
            _guard(eng, i1)
            i2 = eng.tensor_add(g1[:, 16:32], den, gates[:, 16:32])
            _guard(eng, i2)
            i_rd = eng.reciprocal(rden, g1[:, 16:32])
            _guard(eng, i_rd)
            i3 = eng.tensor_mul(E2, Ee, rden)
            _guard(eng, i3)
            i4 = eng.tensor_mul(D2, gates[:, 16:32], rden)
            _guard(eng, i4)
            return eng.tensor_copy(lnv, veps)
        st_den = stage(f"den{u}", "vector", f_den, deps=[st_scan])
        prev[("den", u)] = st_den

        def f_qa(eng, st):
            eng.tensor_mul(t2.rearrange("p (h a) -> p h a", h=16),
                           pFG[:].rearrange("p (h a) -> p h a", h=16),
                           E2.unsqueeze(2).broadcast_to([128, 16, 64]))
            eng.tensor_mul(t3.rearrange("p (h a) -> p h a", h=16),
                           pCD[:].rearrange("p (h a) -> p h a", h=16),
                           D2.unsqueeze(2).broadcast_to([128, 16, 64]))
            return eng.tensor_add(xc_t, t2, t3)
        if debug and u == DEBUG_TILE[0]:
            def f_dbg_denc(eng, st):
                eng.tensor_copy(sinbuf[0:128, 0:16], den)
                eng.tensor_copy(sinbuf[0:128, 16:32], rden)
                eng.tensor_copy(sinbuf[0:128, 32:48], E2)
                return eng.tensor_copy(sinbuf[0:128, 48:64], D2)
            st_dbgden_c = stage(f"dbgdenc{u}", "vector", f_dbg_denc, deps=[st_den])
            def f_dbg_den(eng, st):
                eng.dma_start(out=dbg["d_den"], in_=sinbuf[:, 0:64]).then_inc(st.sem, 16)
            stage(f"dbgden{u}", "gpsimd", f_dbg_den, deps=[st_dbgden_c],
                  sem=semD, amt=16, dma=True)
        st_qa = stage(f"qa{u}", "vector", f_qa, deps=[st_den, st_dbgxn])
        prev[("qa", u)] = st_qa

        if debug and u == DEBUG_TILE[0]:
            def f_dbg_Q(eng, st):
                eng.dma_start(out=dbg["d_Q"], in_=xc_t).then_inc(st.sem, 16)
            stage(f"dbgQ{u}", "gpsimd", f_dbg_Q, deps=[st_qa], sem=semD, amt=16, dma=True)

        def f_qtr(eng, st):
            for k in range(8):
                last = eng.transpose(pFG[:, k * 128:(k + 1) * 128],
                                     xc_t[:, k * 128:(k + 1) * 128], iden)
            return last
        st_qtr = stage(f"qtr{u}", "tensor", f_qtr, deps=[st_qa, st_carx])
        prev[("qtr", u)] = st_qtr

        def f_qhi(eng, st):
            return eng.activation(QF_hi.rearrange("p c t -> p (c t)"), pFG[:], AF.Copy)
        st_qhi = stage(f"qhi{u}", "scalar", f_qhi,
                       deps=[st_qtr, prev.get(("po", u - 1))])
        prev[("qhi", u)] = st_qhi

        def f_qlo(eng, st):
            return eng.tensor_sub(QF_lo.rearrange("p c t -> p (c t)"), pFG[:],
                                  QF_hi.rearrange("p c t -> p (c t)"))
        st_qlo = stage(f"qlo{u}", "vector", f_qlo, deps=[st_qtr, st_qhi])
        prev[("qlo", u)] = st_qlo

        def f_qbf(eng, st):
            return eng.tensor_copy(q_bf.rearrange("p c t -> p (c t)"),
                                   QF_hi.rearrange("p c t -> p (c t)"))
        st_qbf = stage(f"qbf{u}", "gpsimd", f_qbf, deps=[st_qhi])

        def f_po(eng, st):
            last = None
            for p in range(3):
                qa_ = (QF_hi, QF_lo, q_bf)[p]
                wv = (wo_hi, wo_hi, wo_lo)[p]
                for k in range(8):
                    first = (p == 0 and k == 0)
                    lastf = (p == 2 and k == 7)
                    for b in range(2):
                        last = eng.matmul(pCD[:, b * 512:(b + 1) * 512], qa_[:, k, :],
                                          wv[:, k, b * 512:(b + 1) * 512],
                                          start=first, stop=lastf)
            return last
        st_po = stage(f"po{u}", "tensor", f_po,
                      deps=[st_qhi, st_qlo, st_qbf, st_wsplit_done])
        prev[("po", u)] = st_po

        def f_ffne(eng, st):
            eng.tensor_scalar(yb, pAB[:], sisk, None, ALU.mult)
            return eng.tensor_add(yb2, yb, bias_rep)
        st_ffne = stage(f"ffne{u}", "vector", f_ffne, deps=[st_ffn])
        prev[("ffne", u)] = st_ffne

        if debug and u == DEBUG_TILE[0]:
            def f_dbg_f1(eng, st):
                return eng.tensor_copy(sinbuf, yb2)
            st_dbgf1 = stage(f"dbgfc{u}", "vector", f_dbg_f1,
                             deps=[st_ffne, st_dbgV_dma])
            def f_dbg_f(eng, st):
                eng.dma_start(out=dbg["d_ffn"], in_=sinbuf).then_inc(st.sem, 16)
            stage(f"dbgf{u}", "gpsimd", f_dbg_f, deps=[st_dbgf1], sem=semD, amt=16, dma=True)

        def f_res(eng, st, xbuf=xbuf, obuf=obuf):
            eng.tensor_add(r1, xbuf, pCD[:])
            return eng.tensor_add(obuf, r1, yb2)
        st_res = stage(f"res{u}", "vector", f_res,
                       deps=[st_po, st_ffne, prev.get(("st", u - 2))])
        prev[("res", u)] = st_res
        prev[("g", u)] = st_g1

        def f_st(eng, st, s=s, j=j, obuf=obuf):
            eng.dma_start(out=dOut[s, j * T:(j + 1) * T, :], in_=obuf).then_inc(st.sem, 16)
        st_st = stage(f"st{u}", "sync", f_st, deps=[st_res], sem=semO[u % 2],
                      amt=16, dma=True)
        prev[("st", u)] = st_st

    total_stores = [ctr[id(semO[0])], ctr[id(semO[1])]]

    eng_map = {}

    def emit_for(eng_name, eng):
        for st in stages:
            if st.eng != eng_name:
                continue
            waits = {}
            for d in st.deps:
                if (not d.is_dma) and d.eng == eng_name:
                    continue
                key = id(d.sem)
                if key not in waits or waits[key][1] < d.val:
                    waits[key] = (d.sem, d.val)
            for sem_, v in waits.values():
                eng.wait_ge(sem_, v)
            last = st.fn(eng, st)
            if not st.is_dma:
                last.then_inc(st.sem, 1)

    with nc.Block() as block:
        @block.sync
        def _(sync):
            emit_for("sync", sync)
            sync.wait_ge(semO[0], total_stores[0])
            if total_stores[1]:
                sync.wait_ge(semO[1], total_stores[1])

        @block.tensor
        def _(tensor):
            emit_for("tensor", tensor)

        @block.vector
        def _(vector):
            emit_for("vector", vector)

        @block.scalar
        def _(scalar):
            emit_for("scalar", scalar)

        @block.gpsimd
        def _(gpsimd):
            emit_for("gpsimd", gpsimd)

    return nc


DEBUG_TILE = [0]
DBG_NO_CARX = [0]
DBG_NO_CARDMA = [False]
DBG_NO_CARRYMM = [False]
DBG_TRACE = []
DBG_MODE = ["full"]
DBG_STARTUP = [True]


def host_inputs(inputs, n_seq=2, n_tiles=16):
    n_tok = n_tiles * T
    k1 = np.asarray(inputs["L0_attn_k1"], np.float32)
    k2 = np.asarray(inputs["L0_attn_k2"], np.float32)
    k3 = np.asarray(inputs["L0_attn_k3"], np.float32)
    vw = np.asarray(inputs["L0_attn_value_weight"], np.float32)
    ow = np.asarray(inputs["L0_attn_output_weight"], np.float32)
    a1 = np.asarray(inputs["L0_attn_a1"], np.float32).reshape(1024)
    a2 = np.asarray(inputs["L0_attn_a2"], np.float32).reshape(1024)
    b1 = np.asarray(inputs["L0_attn_b1"], np.float32).reshape(1024)
    b2 = np.asarray(inputs["L0_attn_b2"], np.float32).reshape(1024)
    cv = np.asarray(inputs["L0_attn_c"], np.float32).reshape(1024)
    fk = np.asarray(inputs["L0_ffn_kernel"])
    bias = np.asarray(inputs["L0_ffn_bias"])
    s_in = np.asarray(inputs["L0_ffn_input_scale"], np.float32).reshape(1, 1)
    s_k = np.asarray(inputs["L0_ffn_kernel_scale"], np.float32).reshape(1, 1)

    vw_cat = np.ascontiguousarray(vw.transpose(1, 0, 2).reshape(1024, 1024))
    wo_cat = np.ascontiguousarray(ow.transpose(0, 2, 1).reshape(1024, 1024))
    kc = np.ascontiguousarray(np.concatenate([k1.T, k2.T, k3.T], axis=1))

    ind = np.zeros((8, 128, 16), np.float32)
    for c in range(8):
        for r_ in range(128):
            ind[c, r_, (c * 128 + r_) // 64] = 1.0
    ind = np.ascontiguousarray(ind.transpose(1, 0, 2))

    def cols(v):
        return np.ascontiguousarray(v.reshape(8, 128).T)

    shared = dict(
        vw=vw_cat, wo=wo_cat, kc=kc, fk=np.ascontiguousarray(fk),
        bias_rep=np.ascontiguousarray(np.broadcast_to(bias, (128, 1024))),
        tri=np.ascontiguousarray(
            (np.arange(128)[:, None] <= np.arange(128)[None, :]).astype(np.float32)),
        iden=np.eye(128, dtype=np.float32),
        ones1=np.ones((1, 128), np.float32),
        ones2=np.ones((2, 128), np.float32),
        n_rep=np.ascontiguousarray(np.broadcast_to(
            np.linspace(0.0, 1.0, 2048, dtype=np.float32), (128, 2048))),
        a1c=cols(a1), b1c=cols(b1), a2c=cols(a2), b2c=cols(b2), cc=cols(cv),
        ind=ind, s_in=s_in, s_k=s_k,
    )
    return shared


def kernel(**inputs):
    x = np.asarray(inputs["L0_attn_x"], np.float32)
    shared = host_inputs(inputs)
    nc = build_program()
    in_maps = []
    for i in range(NC):
        m = dict(shared)
        m["x"] = np.ascontiguousarray(x[2 * i:2 * i + 2])
        in_maps.append(m)
    res = run_bass_kernel_spmd(nc, in_maps, core_ids=list(range(NC)))
    out = np.concatenate([np.asarray(r["out"]) for r in res.results], axis=0)
    return out


# version 02 (final)
# speedup vs baseline: 1.1755x; 1.1755x over previous
import numpy as np
import ml_dtypes
import concourse.bass as bass
import concourse.mybir as mybir
from concourse.bass_utils import run_bass_kernel_spmd

F32 = mybir.dt.float32
F32R = mybir.dt.float32r
BF16 = mybir.dt.bfloat16
F8 = mybir.dt.float8e4
AF = mybir.ActivationFunctionType
ALU = mybir.AluOpType

W = 1024
H = 16
A = 64
NC = 8
NCHUNK = 8
T = 128


class _St:
    __slots__ = ("name", "eng", "fn", "deps", "sem", "amt", "val", "is_dma")


def build_program(n_seq=2, n_tiles=16, debug=False):
    n_tok = n_tiles * T
    nc = bass.Bass("TRN2", target_bir_lowering=False, detect_race_conditions=False)

    dX = nc.dram_tensor("x", [n_seq, n_tok, W], F32, kind="ExternalInput").ap()
    dOut = nc.dram_tensor("out", [n_seq, n_tok, W], F32, kind="ExternalOutput").ap()
    dVw = nc.dram_tensor("vw", [W, 1024], F32, kind="ExternalInput").ap()
    dWo = nc.dram_tensor("wo", [1024, W], F32, kind="ExternalInput").ap()
    dKc = nc.dram_tensor("kc", [W, 48], F32, kind="ExternalInput").ap()
    dFk = nc.dram_tensor("fk", [W, 1024], BF16, kind="ExternalInput").ap()
    dBias = nc.dram_tensor("bias_rep", [128, 1024], BF16, kind="ExternalInput").ap()
    dTri = nc.dram_tensor("tri", [128, 128], F32R, kind="ExternalInput").ap()
    dIden = nc.dram_tensor("iden", [128, 128], F32, kind="ExternalInput").ap()
    dOnes1 = nc.dram_tensor("ones1", [1, 128], F32, kind="ExternalInput").ap()
    dOnes2 = nc.dram_tensor("ones2", [2, 128], F32R, kind="ExternalInput").ap()
    dNrep = nc.dram_tensor("n_rep", [128, 2048], F32, kind="ExternalInput").ap()
    dA1 = nc.dram_tensor("a1c", [128, 8], F32, kind="ExternalInput").ap()
    dB1 = nc.dram_tensor("b1c", [128, 8], F32, kind="ExternalInput").ap()
    dA2 = nc.dram_tensor("a2c", [128, 8], F32, kind="ExternalInput").ap()
    dB2 = nc.dram_tensor("b2c", [128, 8], F32, kind="ExternalInput").ap()
    dCc = nc.dram_tensor("cc", [128, 8], F32, kind="ExternalInput").ap()
    dInd = nc.dram_tensor("ind", [128, 8, 16], F32, kind="ExternalInput").ap()
    dSin = nc.dram_tensor("s_in", [1, 1], F32, kind="ExternalInput").ap()
    dSk = nc.dram_tensor("s_k", [1, 1], F32, kind="ExternalInput").ap()

    dbg = {}
    if debug:
        for nm, shp, dt in [("d_xn", [128, 1024], F32), ("d_gates", [128, 48], F32),
                            ("d_S", [128, 1040], F32), ("d_Q", [128, 1024], F32),
                            ("d_V", [128, 1024], F32), ("d_p1", [128, 768], F32),
                            ("d_p2e", [128, 256], F32), ("d_ffn", [128, 1024], F32),
                            ("d_den", [128, 64], F32)]:
            dbg[nm] = nc.dram_tensor(nm, shp, dt, kind="ExternalOutput").ap()

    sb = nc.alloc_sbuf_tensor
    vw_hi = sb("vw_hi", [128, 8, 1024], F32R).ap()
    vw_lo = sb("vw_lo", [128, 8, 1024], BF16).ap()
    wo_hi = sb("wo_hi", [128, 8, 1024], F32R).ap()
    wo_lo = sb("wo_lo", [128, 8, 1024], BF16).ap()
    kq = sb("kq", [128, 8, 1024], F8).ap()
    kc_hi = sb("kc_hi", [128, 8, 48], F32R).ap()
    kc_lo = sb("kc_lo", [128, 8, 48], BF16).ap()
    kc_f32 = sb("kc_f32", [128, 8, 48], F32).ap()
    wst = sb("wst", [128, 2, 1024], F32).ap()
    fst = sb("fst", [128, 2, 1024], BF16).ap()

    xb = [sb(f"xb{i}", [128, 1024], F32).ap() for i in range(2)]
    ob = [sb(f"ob{i}", [128, 1024], F32).ap() for i in range(2)]
    xc_t = sb("xc_t", [128, 1024], F32).ap()
    xn_tm = sb("xn_tm", [128, 1024], F32).ap()
    sums = sb("sums", [128, 1], F32).ap()
    negmean = sb("negmean", [128, 1], F32).ap()
    varsum = sb("varsum", [128, 1], F32).ap()
    veps = sb("veps", [128, 1], F32).ap()
    lnv = sb("lnv", [128, 1], F32).ap()
    rstd = sb("rstd", [128, 1], F32).ap()
    xnF_hi = sb("xnF_hi", [128, 8, 128], F32R).ap()
    xnF_lo = sb("xnF_lo", [128, 8, 128], F32R).ap()
    t_bf = sb("t_bf", [128, 8, 128], BF16).ap()
    xq = sb("xq", [128, 8, 128], F8).ap()
    gates = sb("gates", [128, 48], F32).ap()
    g1 = sb("g1", [128, 48], F32).ap()
    Ee = sb("Ee", [128, 16], F32).ap()
    E2 = sb("E2", [128, 16], F32).ap()
    D2 = sb("D2", [128, 16], F32).ap()
    den = sb("den", [128, 16], F32).ap()
    rden = sb("rden", [128, 16], F32).ap()
    G_f32 = sb("G_f32", [128, 1040], F32).ap()
    G_hi = sb("G_hi", [128, 1040], F32R).ap()
    G_lo = sb("G_lo", [128, 1040], F32R).ap()
    t2 = G_f32[:, 0:1024]
    t3 = xn_tm
    QF_hi = sb("QF_hi", [128, 8, 128], F32R).ap()
    QF_lo = sb("QF_lo", [128, 8, 128], F32R).ap()
    q_bf = sb("q_bf", [128, 8, 128], BF16).ap()
    yb = sb("yb", [128, 1024], BF16).ap()
    yb2 = sb("yb2", [128, 1024], BF16).ap()
    r1 = G_f32[:, 0:1024]
    carry2 = sb("carry2", [2, 1040], F32R).ap()
    p1pad = sb("p1pad", [128, 16 * 48], F32).ap()
    p2e_t = sb("p2e_t", [128, 16 * 16], F32).ap()
    bias_rep = sb("bias_rep_s", [128, 1024], BF16).ap()
    tri = sb("tri_s", [128, 128], F32R).ap()
    iden = sb("iden_s", [128, 128], F32).ap()
    ones1 = sb("ones1_s", [1, 128], F32).ap()
    ones2 = sb("ones2_s", [2, 128], F32R).ap()
    n_rep = sb("n_rep_s", [128, 2048], F32).ap()
    a1c = sb("a1c_s", [128, 8], F32).ap()
    b1c = sb("b1c_s", [128, 8], F32).ap()
    a2c = sb("a2c_s", [128, 8], F32).ap()
    b2c = sb("b2c_s", [128, 8], F32).ap()
    ccs = sb("cc_s", [128, 8], F32).ap()
    ind = sb("ind_s", [128, 8, 16], F32).ap()
    csel = sb("csel", [128, 8, 16], F32).ap()
    sinbuf = sb("sinbuf", [128, 1024], F32).ap()
    p_hm = sinbuf[0:16, :]
    wst_flat = wst.rearrange("p c m -> p (c m)")
    s_in_b = sb("s_in_b", [128, 1], F32).ap()
    s_k_b = sb("s_k_b", [128, 1], F32).ap()
    r_sin = sb("r_sin", [128, 1], F32).ap()
    r_sk = sb("r_sk", [128, 1], F32).ap()
    sisk = sb("sisk", [128, 1], F32).ap()
    s_st = sb("s_st", [1, 2], F32).ap()

    pAB = nc.alloc_psum_tensor("pAB", [128, 1024], F32).ap()
    pCD = nc.alloc_psum_tensor("pCD", [128, 1024], F32).ap()
    pE = nc.alloc_psum_tensor("pE", [128, 512], F32).ap()
    pFG = nc.alloc_psum_tensor("pFG", [128, 1024], F32).ap()
    pH = nc.alloc_psum_tensor("pH", [128, 512], F32).ap()

    semT = nc.alloc_semaphore("semT")
    semV = nc.alloc_semaphore("semV")
    semS = nc.alloc_semaphore("semS")
    semG = nc.alloc_semaphore("semG")
    semX = [nc.alloc_semaphore(f"semX{i}") for i in range(2)]
    semWc = nc.alloc_semaphore("semWc")
    semWw = [nc.alloc_semaphore(f"semWw{i}") for i in range(2)]
    semWf = [nc.alloc_semaphore(f"semWf{i}") for i in range(2)]
    semO = [nc.alloc_semaphore(f"semO{i}") for i in range(2)]
    semC = nc.alloc_semaphore("semC")
    semD = nc.alloc_semaphore("semD")
    semA = nc.alloc_semaphore("semA")
    _aux = [0]

    def _guard(eng, producer_inst):
        _aux[0] += 1
        producer_inst.then_inc(semA, 1)
        eng.wait_ge(semA, _aux[0])
    sems = {"tensor": semT, "vector": semV, "scalar": semS, "gpsimd": semG}

    stages = []
    from collections import defaultdict
    ctr = defaultdict(int)

    def stage(name, eng, fn, deps=(), sem=None, amt=1, dma=False):
        st = _St()
        st.name, st.eng, st.fn = name, eng, fn
        st.deps = [d for d in deps if d is not None]
        st.sem = sem if sem is not None else sems[eng]
        st.amt, st.is_dma = amt, dma
        ctr[id(st.sem)] += amt
        st.val = ctr[id(st.sem)]
        stages.append(st)
        DBG_TRACE.append((name, eng))
        return st

    st_const = st_bcast = st_scalprep = st_kq = st_kc_hi = st_kc_lo = None
    st_wsplit_done = st_csel = st_p1ev = st_ptev = None
    if not DBG_STARTUP[0]:
        stages_startup_skipped = True
    def f_const_loads(eng, st):
        for dst_, src_ in [(tri, dTri), (iden, dIden), (ones1, dOnes1), (ones2, dOnes2),
                           (n_rep, dNrep), (a1c, dA1), (b1c, dB1), (a2c, dA2),
                           (b2c, dB2), (ccs, dCc), (ind, dInd),
                           (bias_rep, dBias),
                           (kc_f32, dKc.rearrange("(c p) k -> p c k", p=128))]:
            eng.dma_start(out=dst_, in_=src_).then_inc(st.sem, 16)
        eng.dma_start(out=s_st[:, 0:1], in_=dSin).then_inc(st.sem, 16)
        eng.dma_start(out=s_st[:, 1:2], in_=dSk).then_inc(st.sem, 16)
    if DBG_STARTUP[0]:
        st_const = stage("const_loads", "scalar", f_const_loads, sem=semWc, amt=16 * 15, dma=True)

    if DBG_STARTUP[0]:
        def f_bcast(eng, st):
            eng.matmul(pH[:, 0:1], ones1, s_st[:, 0:1], start=True, stop=True)
            return eng.matmul(pH[:, 1:2], ones1, s_st[:, 1:2], start=True, stop=True)
        st_bcast = stage("bcast", "tensor", f_bcast, deps=[st_const])

        def f_scalprep(eng, st):
            eng.tensor_copy(s_in_b, pH[:, 0:1])
            i_c = eng.tensor_copy(s_k_b, pH[:, 1:2])
            _guard(eng, i_c)
            i_r1 = eng.reciprocal(r_sin, s_in_b)
            _guard(eng, i_r1)
            i_r2 = eng.reciprocal(r_sk, s_k_b)
            _guard(eng, i_r2)
            eng.tensor_mul(sisk, s_in_b, s_k_b)
            eng.memset(p1pad, 0.0)
            return eng.memset(carry2.bitcast(F32), 0.0)
        st_scalprep = stage("scalprep", "vector", f_scalprep, deps=[st_bcast])

        st_fdma, st_kq_c = {}, {}
        for c in range(8):
            def f_fdma(eng, st, c=c):
                eng.dma_start(out=fst[:, c % 2, :],
                              in_=dFk[c * 128:(c + 1) * 128, :]).then_inc(st.sem, 16)
            st_fdma[c] = stage(f"fdma{c}", "scalar", f_fdma,
                               deps=[st_kq_c.get(c - 2)], sem=semWf[c % 2], amt=16, dma=True)

            def f_kq(eng, st, c=c):
                return eng.activation(kq[:, c, :], fst[:, c % 2, :], AF.Copy, scale=r_sk)
            st_kq_c[c] = stage(f"kq{c}", "scalar", f_kq, deps=[st_fdma[c], st_scalprep])
        st_kq = st_kq_c[7]

        def f_kc_hi(eng, st):
            return eng.tensor_copy(kc_hi, kc_f32)
        st_kc_hi = stage("kc_hi", "gpsimd", f_kc_hi, deps=[st_const])

        def f_kc_lo(eng, st):
            return eng.tensor_sub(kc_lo, kc_f32, kc_hi)
        st_kc_lo = stage("kc_lo", "vector", f_kc_lo, deps=[st_kc_hi, st_scalprep])

        st_wdma, st_whi, st_wlo = {}, {}, {}
        for gi, (dsrc, whi, wlo) in enumerate([(dVw, vw_hi, vw_lo), (dWo, wo_hi, wo_lo)]):
            for c in range(8):
                i = gi * 8 + c
                par = i % 2
                prev_lo = st_wlo.get(i - 2)

                def f_wdma(eng, st, dsrc=dsrc, c=c, par=par):
                    eng.dma_start(out=wst[:, par, :],
                                  in_=dsrc[c * 128:(c + 1) * 128, :]).then_inc(st.sem, 16)
                st_wdma[i] = stage(f"wdma{i}", "scalar", f_wdma,
                                   deps=[prev_lo], sem=semWw[i % 2], amt=16, dma=True)

                def f_whi(eng, st, whi=whi, c=c, par=par):
                    return eng.activation(whi[:, c, :], wst[:, par, :], AF.Copy)
                st_whi[i] = stage(f"whi{i}", "scalar", f_whi, deps=[st_wdma[i]])

                def f_wlo(eng, st, whi=whi, wlo=wlo, c=c, par=par):
                    return eng.tensor_sub(wlo[:, c, :], wst[:, par, :], whi[:, c, :])
                st_wlo[i] = stage(f"wlo{i}", "vector", f_wlo, deps=[st_whi[i]])
        st_wsplit_done = st_wlo[15]

        def f_csel(eng, st):
            for c in range(8):
                last = eng.tensor_scalar(csel[:, c, :], ind[:, c, :], ccs[:, c:c + 1],
                                         None, ALU.mult)
            return last
        st_csel = stage("csel", "vector", f_csel, deps=[st_const])

        st_sin, st_pred = {}, {}
        st_p1ev = None
        st_ptr_prev = None
        p1_slices = [pAB[0:16, 0:512], pAB[0:16, 512:1024], pCD[0:16, 0:512], pCD[0:16, 512:1024]]
        st_ptev = None
        for pi, (ac, bc) in enumerate([(a1c, b1c), (a2c, b2c)]):
            for c8 in range(16):
                c, half = divmod(c8, 2)
                prev_pred = (st_pred.get((pi, c8 - 1))
                             or (st_pred.get((pi - 1, 15)) if pi else None))

                def f_sin(eng, st, ac=ac, bc=bc, c=c, half=half):
                    return eng.activation(sinbuf, n_rep[:, half * 1024:(half + 1) * 1024],
                                          AF.Sin, scale=ac[:, c:c + 1], bias=bc[:, c:c + 1])
                st_sin[(pi, c8)] = stage(f"sin{pi}_{c8}", "scalar", f_sin,
                                         deps=[st_const, prev_pred,
                                               st_p1ev if (pi == 1 and c8 == 0) else None])

                def f_pred(eng, st, pi=pi, c=c, half=half):
                    for q in range(2):
                        last = eng.matmul(p1_slices[half * 2 + q], csel[:, c, :],
                                          sinbuf[:, q * 512:(q + 1) * 512],
                                          start=(c == 0), stop=(c == 7))
                    return last
                st_pred[(pi, c8)] = stage(f"pred{pi}_{c8}", "tensor", f_pred,
                                          deps=[st_sin[(pi, c8)], st_csel,
                                                st_bcast if (pi == 0 and c8 == 0) else None,
                                                st_p1ev if (pi == 1 and c8 == 0) else None])
            for half in range(2):
                def f_pev(eng, st, half=half):
                    eng.tensor_copy(p_hm[:, 0:512], p1_slices[half * 2])
                    return eng.tensor_copy(p_hm[:, 512:1024], p1_slices[half * 2 + 1])
                st_pev = stage(f"pev{pi}_{half}", "vector", f_pev,
                               deps=[st_pred[(pi, 15)], st_ptr_prev])
                if pi == 1:
                    def f_p2clip(eng, st):
                        return eng.tensor_scalar(p_hm, p_hm, 20.0, -20.0, ALU.min, ALU.max)
                    st_p2clip = stage(f"p2clip{half}", "vector", f_p2clip, deps=[st_pev])

                    def f_p2exp(eng, st):
                        return eng.activation(p_hm, p_hm, AF.Exp)
                    st_ready = stage(f"p2exp{half}", "scalar", f_p2exp, deps=[st_p2clip])
                else:
                    st_ready = st_pev

                def f_ptr(eng, st):
                    for j in range(8):
                        last = eng.transpose(pH[:, j * 16:(j + 1) * 16],
                                             p_hm[:, j * 128:(j + 1) * 128],
                                             iden[0:16, 0:16])
                    return last
                st_ptr_prev = stage(f"ptr{pi}_{half}", "tensor", f_ptr,
                                    deps=[st_ready, st_ptev])

                def f_ptev(eng, st, pi=pi, half=half):
                    for j8 in range(8):
                        j = half * 8 + j8
                        if pi == 0:
                            last = eng.tensor_copy(p1pad[:, j * 48:j * 48 + 16],
                                                   pH[:, j8 * 16:(j8 + 1) * 16])
                        else:
                            last = eng.tensor_copy(p2e_t[:, j * 16:(j + 1) * 16],
                                                   pH[:, j8 * 16:(j8 + 1) * 16])
                    return last
                st_ptev = stage(f"ptev{pi}_{half}", "vector", f_ptev,
                                deps=[st_ptr_prev, st_scalprep])
            if pi == 0:
                st_p1ev = st_ptev

        if debug:
            def f_dbg_p(eng, st):
                eng.dma_start(out=dbg["d_p1"], in_=p1pad).then_inc(st.sem, 16)
                eng.dma_start(out=dbg["d_p2e"], in_=p2e_t).then_inc(st.sem, 16)
            stage("dbg_p", "gpsimd", f_dbg_p, deps=[st_ptev], sem=semD, amt=32, dma=True)

    n_tiles_total = n_seq * n_tiles
    prev = {}

    def f_ldx0(eng, st):
        eng.dma_start(out=xb[0], in_=dX[0, 0:T, :]).then_inc(st.sem, 16)
    prev[("ldx", 0)] = stage("ldx0", "sync", f_ldx0, sem=semX[0], amt=16, dma=True)

    def tail_stages(v):
        def f_den(eng, st, v=v):
            i1 = eng.scalar_tensor_tensor(den, pH[:, 0:16], 1.0, Ee, ALU.mult, ALU.mult)
            _guard(eng, i1)
            i2 = eng.tensor_add(g1[:, 16:32], den, gates[:, 16:32])
            _guard(eng, i2)
            i_rd = eng.reciprocal(rden, g1[:, 16:32])
            _guard(eng, i_rd)
            i3 = eng.tensor_mul(E2, Ee, rden)
            _guard(eng, i3)
            i4 = eng.tensor_mul(D2, gates[:, 16:32], rden)
            _guard(eng, i4)
            return eng.tensor_copy(den[:, 0:1], rden[:, 0:1])
        st_den = stage(f"den{v}", "vector", f_den,
                       deps=[prev[("scan", v)], prev[("G", v)]])
        prev[("den", v)] = st_den

        qf = ob[v % 2]
        def f_qa(eng, st, qf=qf):
            eng.tensor_mul(t2.rearrange("p (h a) -> p h a", h=16),
                           pFG[:].rearrange("p (h a) -> p h a", h=16),
                           E2.unsqueeze(2).broadcast_to([128, 16, 64]))
            eng.tensor_mul(t3.rearrange("p (h a) -> p h a", h=16),
                           pCD[:].rearrange("p (h a) -> p h a", h=16),
                           D2.unsqueeze(2).broadcast_to([128, 16, 64]))
            return eng.tensor_add(qf, t2, t3)
        st_qa = stage(f"qa{v}", "vector", f_qa,
                      deps=[st_den, prev.get(("st", v - 2))])
        prev[("qa", v)] = st_qa

        def f_qbf(eng, st, qf=qf):
            return eng.tensor_copy(q_bf.rearrange("p c t -> p (c t)"), qf)
        st_qbf = stage(f"qbf{v}", "gpsimd", f_qbf,
                       deps=[st_qa, prev.get(("po", v - 1))])
        prev[("qbf", v)] = st_qbf

    def mid_stages(v):
        qf = ob[v % 2]
        def f_qtr(eng, st, qf=qf):
            for k in range(8):
                last = eng.transpose(pFG[:, k * 128:(k + 1) * 128],
                                     qf[:, k * 128:(k + 1) * 128], iden)
            return last
        st_qtr = stage(f"qtr{v}", "tensor", f_qtr,
                       deps=[prev[("qa", v)], prev[("carx", v)]])
        prev[("qtr", v)] = st_qtr

        def f_qhi(eng, st):
            return eng.activation(QF_hi.rearrange("p c t -> p (c t)"), pFG[:], AF.Copy)
        st_qhi = stage(f"qhi{v}", "scalar", f_qhi,
                       deps=[st_qtr, prev.get(("po", v - 1))])
        prev[("qhi", v)] = st_qhi

        def f_qlo(eng, st):
            return eng.tensor_sub(QF_lo.rearrange("p c t -> p (c t)"), pFG[:],
                                  QF_hi.rearrange("p c t -> p (c t)"))
        st_qlo = stage(f"qlo{v}", "vector", f_qlo, deps=[st_qtr, st_qhi])
        prev[("qlo", v)] = st_qlo

    def po_res_stages(v):
        def f_po(eng, st):
            last = None
            for p in range(3):
                qa_ = (QF_hi, QF_lo, q_bf)[p]
                wv = (wo_hi, wo_hi, wo_lo)[p]
                for k in range(8):
                    first = (p == 0 and k == 0)
                    lastf = (p == 2 and k == 7)
                    for b in range(2):
                        last = eng.matmul(pCD[:, b * 512:(b + 1) * 512], qa_[:, k, :],
                                          wv[:, k, b * 512:(b + 1) * 512],
                                          start=first, stop=lastf)
            return last
        st_po = stage(f"po{v}", "tensor", f_po,
                      deps=[prev[("qhi", v)], prev[("qlo", v)], prev[("qbf", v)],
                            st_wsplit_done])
        prev[("po", v)] = st_po

        def f_res(eng, st, v=v):
            eng.tensor_add(r1, xb[v % 2], pCD[:])
            return eng.tensor_add(ob[v % 2], r1, yb2)
        st_res = stage(f"res{v}", "vector", f_res,
                       deps=[st_po, prev[("ffne", v)], prev.get(("st", v - 2)),
                             prev[("qbf", v)]])
        prev[("res", v)] = st_res

        s_, j_ = divmod(v, n_tiles)
        def f_st(eng, st, s_=s_, j_=j_, v=v):
            eng.dma_start(out=dOut[s_, j_ * T:(j_ + 1) * T, :],
                          in_=ob[v % 2]).then_inc(st.sem, 16)
        st_st = stage(f"st{v}", "sync", f_st, deps=[st_res], sem=semO[v % 2],
                      amt=16, dma=True)
        prev[("st", v)] = st_st

    for u in range(n_tiles_total + 1):
        last_u = u >= n_tiles_total
        if not last_u:
            s, j = divmod(u, n_tiles)
            first_of_seq = (j == 0)

        if not last_u:
            def f_ln1(eng, st, xbuf=xb[u % 2]):
                i_sum = eng.reduce_sum(sums, xbuf, mybir.AxisListType.X)
                _guard(eng, i_sum)
                i_nm = eng.tensor_scalar(negmean, sums, -1.0 / 1024.0, None, ALU.mult)
                _guard(eng, i_nm)
                eng.tensor_scalar(xc_t, xbuf, negmean, None, ALU.add)
                i_sq = eng.scalar_tensor_tensor(yb, xc_t, 1.0, xc_t, ALU.mult,
                                                ALU.mult, accum_out=varsum)
                _guard(eng, i_sq)
                return eng.tensor_scalar(veps, varsum, 1.0 / 1024.0, 1e-5,
                                         ALU.mult, ALU.add)
            st_ln1 = stage(f"ln1{u}", "vector", f_ln1,
                           deps=[prev.get(("ldx", u)), prev.get(("ln2", u - 1)),
                                 st_ptev if u == 0 else None,
                                 prev.get(("ffne", u - 1))])

            def f_ln2(eng, st):
                eng.activation(lnv, veps, AF.Ln)
                return eng.activation(rstd, lnv, AF.Exp, scale=-0.5)
            st_ln2 = stage(f"ln2{u}", "scalar", f_ln2, deps=[st_ln1])
            prev[("ln2", u)] = st_ln2

        if u > 0:
            v = u - 1
            def f_ffne(eng, st):
                eng.tensor_scalar(yb, pAB[:], sisk, None, ALU.mult)
                return eng.tensor_add(yb2, yb, bias_rep)
            st_ffne = stage(f"ffne{v}", "vector", f_ffne,
                            deps=[prev[("ffn", v)], prev.get(("ln2", u))])
            prev[("ffne", v)] = st_ffne
            tail_stages(v)

        if not last_u:
            def f_ln3(eng, st):
                return eng.tensor_scalar(xn_tm, xc_t, rstd, None, ALU.mult)
            st_ln3 = stage(f"ln3{u}", "vector", f_ln3,
                           deps=[st_ln2, prev.get(("tr", u - 1))])

            def f_tr(eng, st):
                for k in range(8):
                    last = eng.transpose(pAB[:, k * 128:(k + 1) * 128],
                                         xn_tm[:, k * 128:(k + 1) * 128], iden)
                return last
            st_tr = stage(f"tr{u}", "tensor", f_tr,
                          deps=[st_ln3, prev.get(("ffne", u - 1)),
                                st_ptev if u == 0 else None])
            prev[("tr", u)] = st_tr

            def f_hi(eng, st):
                eng.activation(xnF_hi.rearrange("p c t -> p (c t)"), pAB[:], AF.Copy)
                eng.activation(t_bf.rearrange("p c t -> p (c t)"), pAB[:], AF.Copy)
                return eng.activation(xq.rearrange("p c t -> p (c t)"),
                                      t_bf.rearrange("p c t -> p (c t)"), AF.Copy,
                                      scale=r_sin)
            st_hi = stage(f"hi{u}", "scalar", f_hi,
                          deps=[st_tr, prev.get(("vz", u - 1))])
            prev[("hi", u)] = st_hi

            def f_lo(eng, st):
                return eng.tensor_sub(xnF_lo.rearrange("p c t -> p (c t)"), pAB[:],
                                      xnF_hi.rearrange("p c t -> p (c t)"))
            st_lo = stage(f"lo{u}", "vector", f_lo,
                          deps=[st_tr, st_hi, prev.get(("vz", u - 1))])
            prev[("lo", u)] = st_lo

        if u > 0:
            mid_stages(u - 1)

        if not last_u:
            def f_ffn(eng, st):
                for k in range(8):
                    for b in range(2):
                        last = eng.matmul(pAB[:, b * 512:(b + 1) * 512], xq[:, k, :],
                                          kq[:, k, b * 512:(b + 1) * 512],
                                          start=(k == 0), stop=(k == 7))
                return last
            st_ffn = stage(f"ffn{u}", "tensor", f_ffn,
                           deps=[prev[("hi", u)], prev[("lo", u)], st_kq])
            prev[("ffn", u)] = st_ffn

        if u > 0:
            po_res_stages(u - 1)

        if u + 1 < n_tiles_total:
            def f_ldx(eng, st, u1=u + 1):
                s1, j1 = divmod(u1, n_tiles)
                eng.dma_start(out=xb[u1 % 2],
                              in_=dX[s1, j1 * T:(j1 + 1) * T, :]).then_inc(st.sem, 16)
            st_ldx1 = stage(f"ldx{u + 1}", "sync", f_ldx,
                            deps=[prev.get(("res", u - 1))],
                            sem=semX[(u + 1) % 2], amt=16, dma=True)
            prev[("ldx", u + 1)] = st_ldx1

        if not last_u:
            def f_vz(eng, st):
                last = None
                for p in range(3):
                    xa = (xnF_hi, xnF_lo, t_bf)[p]
                    wv = (vw_hi, vw_hi, vw_lo)[p]
                    wk = (kc_hi, kc_hi, kc_lo)[p]
                    for k in range(8):
                        first = (p == 0 and k == 0)
                        lastf = (p == 2 and k == 7)
                        for b in range(2):
                            last = eng.matmul(pCD[:, b * 512:(b + 1) * 512],
                                              xa[:, k, :],
                                              wv[:, k, b * 512:(b + 1) * 512],
                                              start=first, stop=lastf)
                        last = eng.matmul(pE[:, 0:48], xa[:, k, :], wk[:, k, :],
                                          start=first, stop=lastf)
                return last
            st_vz = stage(f"vz{u}", "tensor", f_vz,
                          deps=[prev[("hi", u)], prev[("lo", u)], st_wsplit_done,
                                st_kc_lo, prev.get(("res", u - 1)),
                                prev.get(("g1", u - 1)), prev.get(("den", u - 1)),
                                st_ptev if u == 0 else None])
            prev[("vz", u)] = st_vz

            def f_g1(eng, st, j=j):
                i_a = eng.tensor_add(g1, pE[:, 0:48], p1pad[:, j * 48:(j + 1) * 48])
                _guard(eng, i_a)
                return eng.tensor_scalar(g1, g1, 20.0, -20.0, ALU.min, ALU.max)
            st_g1 = stage(f"g1{u}", "vector", f_g1, deps=[st_vz])
            prev[("g1", u)] = st_g1

            def f_g2(eng, st):
                return eng.activation(gates, g1, AF.Exp)
            st_g2 = stage(f"g2{u}", "scalar", f_g2, deps=[st_g1])

            def f_G(eng, st, j=j):
                i_ee = eng.tensor_mul(Ee, gates[:, 32:48], p2e_t[:, j * 16:(j + 1) * 16])
                _guard(eng, i_ee)
                eng.tensor_mul(G_f32[:, 0:1024].rearrange("p (h a) -> p h a", h=16),
                               pCD[:].rearrange("p (h a) -> p h a", h=16),
                               gates[:, 0:16].unsqueeze(2).broadcast_to([128, 16, 64]))
                return eng.tensor_copy(G_f32[:, 1024:1040], gates[:, 0:16])
            st_G = stage(f"G{u}", "vector", f_G,
                         deps=[st_g2, prev.get(("ghl", u - 1)),
                               prev.get(("res", u - 1))])
            prev[("G", u)] = st_G

            def f_Ghl(eng, st):
                eng.tensor_copy(G_hi, G_f32)
                return eng.tensor_sub(G_lo, G_f32, G_hi)
            st_Ghl = stage(f"Ghl{u}", "gpsimd", f_Ghl,
                           deps=[st_G, prev.get(("scan", u - 1)),
                                 prev.get(("carry", u - 1))])
            prev[("ghl", u)] = st_Ghl

            def f_scan(eng, st, first_of_seq=first_of_seq):
                cuts = [(0, 512, pFG[:, 0:512]), (512, 1024, pFG[:, 512:1024]),
                        (1024, 1040, pH[:, 0:16])]
                npass = 2 if first_of_seq else 3
                last = None
                for p in range(npass):
                    for (c0, c1, dst) in cuts:
                        if p < 2:
                            src = (G_hi, G_lo)[p]
                            last = eng.matmul(dst, tri, src[:, c0:c1],
                                              start=(p == 0), stop=(p == npass - 1))
                        else:
                            last = eng.matmul(dst, ones2, carry2[:, c0:c1],
                                              start=False, stop=True)
                return eng.matmul(pE[0:1, 64:66], t_bf[:, 0, 0:1], t_bf[:, 0, 0:2],
                                  start=True, stop=True)
            st_scan = stage(f"scan{u}", "tensor", f_scan,
                            deps=[st_Ghl, prev.get(("carry", u - 1)),
                                  prev.get(("qa", u - 1)), prev.get(("den", u - 1)),
                                  prev.get(("qhi", u - 1)), prev.get(("qlo", u - 1)),
                                  prev.get(("carx", u - 1)),
                                  st_ptev if u == 0 else None])
            prev[("scan", u)] = st_scan

            if j < n_tiles - 1:
                def f_carx(eng, st):
                    eng.tensor_copy(n_rep[96:128, 0:1024], pFG[96:128, :])
                    i_c = eng.tensor_copy(n_rep[96:128, 1024:1040], pH[96:128, 0:16])
                    _guard(eng, i_c)
                    i_h = eng.tensor_copy(G_hi[96:128, :], n_rep[96:128, 0:1040])
                    _guard(eng, i_h)
                    return eng.tensor_sub(G_lo[96:128, :], n_rep[96:128, 0:1040],
                                          G_hi[96:128, :])
                st_carx = stage(f"carx{u}", "vector", f_carx, deps=[st_scan])

                def f_cardma(eng, st):
                    eng.dma_start(out=carry2[0:1, :],
                                  in_=G_hi[127:128, 0:1040]).then_inc(st.sem, 16)
                    eng.dma_start(out=carry2[1:2, :],
                                  in_=G_lo[127:128, 0:1040]).then_inc(st.sem, 16)
                st_carry = stage(f"cardma{u}", "gpsimd", f_cardma, deps=[st_carx],
                                 sem=semC, amt=32, dma=True)
            else:
                def f_carx0(eng, st):
                    return eng.memset(carry2.bitcast(F32), 0.0)
                st_carx = stage(f"carx{u}", "vector", f_carx0, deps=[st_scan])
                st_carry = st_carx
            prev[("carry", u)] = st_carry
            prev[("carx", u)] = st_carx

    total_stores = [ctr[id(semO[0])], ctr[id(semO[1])]]

    eng_map = {}

    def emit_for(eng_name, eng):
        for st in stages:
            if st.eng != eng_name:
                continue
            waits = {}
            for d in st.deps:
                if (not d.is_dma) and d.eng == eng_name:
                    continue
                key = id(d.sem)
                if key not in waits or waits[key][1] < d.val:
                    waits[key] = (d.sem, d.val)
            for sem_, v in waits.values():
                eng.wait_ge(sem_, v)
            last = st.fn(eng, st)
            if not st.is_dma:
                last.then_inc(st.sem, 1)

    with nc.Block() as block:
        @block.sync
        def _(sync):
            emit_for("sync", sync)
            sync.wait_ge(semO[0], total_stores[0])
            if total_stores[1]:
                sync.wait_ge(semO[1], total_stores[1])

        @block.tensor
        def _(tensor):
            emit_for("tensor", tensor)

        @block.vector
        def _(vector):
            emit_for("vector", vector)

        @block.scalar
        def _(scalar):
            emit_for("scalar", scalar)

        @block.gpsimd
        def _(gpsimd):
            emit_for("gpsimd", gpsimd)

    return nc


DEBUG_TILE = [0]
DBG_NO_CARX = [0]
DBG_NO_CARDMA = [False]
DBG_NO_CARRYMM = [False]
DBG_TRACE = []
DBG_MODE = ["full"]
DBG_STARTUP = [True]


def host_inputs(inputs, n_seq=2, n_tiles=16):
    n_tok = n_tiles * T
    k1 = np.asarray(inputs["L0_attn_k1"], np.float32)
    k2 = np.asarray(inputs["L0_attn_k2"], np.float32)
    k3 = np.asarray(inputs["L0_attn_k3"], np.float32)
    vw = np.asarray(inputs["L0_attn_value_weight"], np.float32)
    ow = np.asarray(inputs["L0_attn_output_weight"], np.float32)
    a1 = np.asarray(inputs["L0_attn_a1"], np.float32).reshape(1024)
    a2 = np.asarray(inputs["L0_attn_a2"], np.float32).reshape(1024)
    b1 = np.asarray(inputs["L0_attn_b1"], np.float32).reshape(1024)
    b2 = np.asarray(inputs["L0_attn_b2"], np.float32).reshape(1024)
    cv = np.asarray(inputs["L0_attn_c"], np.float32).reshape(1024)
    fk = np.asarray(inputs["L0_ffn_kernel"])
    bias = np.asarray(inputs["L0_ffn_bias"])
    s_in = np.asarray(inputs["L0_ffn_input_scale"], np.float32).reshape(1, 1)
    s_k = np.asarray(inputs["L0_ffn_kernel_scale"], np.float32).reshape(1, 1)

    vw_cat = np.ascontiguousarray(vw.transpose(1, 0, 2).reshape(1024, 1024))
    wo_cat = np.ascontiguousarray(ow.transpose(0, 2, 1).reshape(1024, 1024))
    kc = np.ascontiguousarray(np.concatenate([k1.T, k2.T, k3.T], axis=1))

    ind = np.zeros((8, 128, 16), np.float32)
    for c in range(8):
        for r_ in range(128):
            ind[c, r_, (c * 128 + r_) // 64] = 1.0
    ind = np.ascontiguousarray(ind.transpose(1, 0, 2))

    def cols(v):
        return np.ascontiguousarray(v.reshape(8, 128).T)

    shared = dict(
        vw=vw_cat, wo=wo_cat, kc=kc, fk=np.ascontiguousarray(fk),
        bias_rep=np.ascontiguousarray(np.broadcast_to(bias, (128, 1024))),
        tri=np.ascontiguousarray(
            (np.arange(128)[:, None] <= np.arange(128)[None, :]).astype(np.float32)),
        iden=np.eye(128, dtype=np.float32),
        ones1=np.ones((1, 128), np.float32),
        ones2=np.ones((2, 128), np.float32),
        n_rep=np.ascontiguousarray(np.broadcast_to(
            np.linspace(0.0, 1.0, 2048, dtype=np.float32), (128, 2048))),
        a1c=cols(a1), b1c=cols(b1), a2c=cols(a2), b2c=cols(b2), cc=cols(cv),
        ind=ind, s_in=s_in, s_k=s_k,
    )
    return shared


def kernel(**inputs):
    x = np.asarray(inputs["L0_attn_x"], np.float32)
    shared = host_inputs(inputs)
    nc = build_program()
    in_maps = []
    for i in range(NC):
        m = dict(shared)
        m["x"] = np.ascontiguousarray(x[2 * i:2 * i + 2])
        in_maps.append(m)
    res = run_bass_kernel_spmd(nc, in_maps, core_ids=list(range(NC)))
    out = np.concatenate([np.asarray(r["out"]) for r in res.results], axis=0)
    return out
